# version 10
# baseline (speedup 1.0000x reference)
import numpy as np
import ml_dtypes
import concourse.bass as bass
import concourse.mybir as mybir
from concourse.bass_utils import run_bass_kernel_spmd
from concourse.bass_core_rust import IndirectOffsetOnAxis

F32 = mybir.dt.float32
BF16 = mybir.dt.bfloat16
U32 = mybir.dt.uint32
AX = mybir.AxisListType.X
OP = mybir.AluOpType
AF = mybir.ActivationFunctionType

S = 2048
E = 1024
H = 16
D = 64
DF = 4096
NE = 8
NTC = S // 128
NEC = E // 128
NFC = DF // 128
TT = 512
NTT = S // TT
WSH_F = 2432
WDRAM = 2560
E_SEQ = [0, 2, 4, 6, 1, 3, 5, 7]


class Ctr:
    def __init__(self, h):
        self.h = h
        self.n = 0


def build(debug=False, n_experts=NE):
    nc = bass.Bass("TRN2", target_bir_lowering=False, debug=False)

    x_d = nc.dram_tensor("x", [S, E], F32, kind="ExternalInput")
    wa_d = nc.dram_tensor("W_A", [H, E], F32, kind="ExternalInput")
    wv_d = nc.dram_tensor("W_V", [E, E], F32, kind="ExternalInput")
    wo_d = nc.dram_tensor("W_O", [E, E], F32, kind="ExternalInput")
    bo_d = nc.dram_tensor("b_O", [E], F32, kind="ExternalInput")
    w1_d = nc.dram_tensor("W1", [NE, E, DF], F32, kind="ExternalInput")
    b1_d = nc.dram_tensor("b1", [NE, DF], F32, kind="ExternalInput")
    w2_d = nc.dram_tensor("W2", [NE, DF, E], F32, kind="ExternalInput")
    b2_d = nc.dram_tensor("b2", [NE, E], F32, kind="ExternalInput")
    wg_d = nc.dram_tensor("Wg", [E, NE], F32, kind="ExternalInput")
    bg_d = nc.dram_tensor("bg", [NE], F32, kind="ExternalInput")
    idn_d = nc.dram_tensor("idn", [128, 128], BF16, kind="ExternalInput")
    out_d = nc.dram_tensor("out", [S, E], F32, kind="ExternalOutput")
    wdr_d = nc.dram_tensor("wdr", [H, WDRAM], BF16)
    w2bf_d = nc.dram_tensor("w2bf", [NE, DF, E], BF16)

    taps = {}
    if debug:
        taps["xn_fm"] = nc.dram_tensor("t_xn_fm", [128, NEC * S], BF16, kind="ExternalOutput")
        taps["z_bf"] = nc.dram_tensor("t_z_bf", [H, S], BF16, kind="ExternalOutput")
        taps["v_tm"] = nc.dram_tensor("t_v_tm", [128, NTC * E], BF16, kind="ExternalOutput")
        taps["mix"] = nc.dram_tensor("t_mix", [128, NEC * S], BF16, kind="ExternalOutput")
        taps["comb"] = nc.dram_tensor("t_comb", [128, 128], F32, kind="ExternalOutput")
        taps["acc_attn"] = nc.dram_tensor("t_acc_attn", [128, NTC * E], F32, kind="ExternalOutput")
        taps["a_e0"] = nc.dram_tensor("t_a_e0", [128, NFC * TT], BF16, kind="ExternalOutput")
        taps["lnst"] = nc.dram_tensor("t_lnst", [128, NTC * 8], F32, kind="ExternalOutput")

    TOTAL = 212_768
    backing = nc.alloc_sbuf_tensor("mk_backing", [128, TOTAL], mybir.dt.uint8)
    sb_base = nc.lookup_mloc(backing).addr

    def place(name, shape, dtype, off):
        return nc.alloc_sbuf_tensor_at(name, shape, dtype, offset=sb_base + off)

    off = 0
    def nxt(nbytes, align=32):
        nonlocal off
        off = (off + align - 1) // align * align
        o = off
        off += nbytes
        return o

    x_acc = place("x_acc", [128, NTC * E], F32, nxt(NTC * E * 4))
    xn_fm = place("xn_fm", [128, NEC * S], BF16, nxt(NEC * S * 2))
    idn = place("idn_sb", [128, 128], BF16, nxt(256))
    ones_bf = place("ones_bf", [1, 128], BF16, nxt(256))
    ones_f = place("ones_f", [1, 128], F32, nxt(512))
    W_AT = place("W_AT", [128, 128], BF16, nxt(256))
    Wg_bf = place("Wg_bf", [128, 64], BF16, nxt(128))
    bo_bf = place("bo_bf", [1, E], BF16, nxt(2048))
    bg_sb = place("bg_sb", [1, NE], F32, nxt(32))
    bgb = place("bgb", [128, NE], F32, nxt(32))
    b1c = place("b1c", [128, 32 * NE], F32, nxt(32 * NE * 4))
    b2_bf = place("b2_bf", [NE, E], BF16, nxt(2048))
    comb = place("comb", [128, 128], F32, nxt(512))
    comb_bf = place("comb_bf", [128, 8], BF16, nxt(32))
    combT = place("combT", [8, 128], BF16, nxt(256))
    idx_u = place("idx_u", [128, 1], U32, nxt(32))
    sm = place("smalls", [128, 96], F32, nxt(384))
    ones8 = place("ones8", [8, 8], F32, nxt(32))
    idn8 = place("idn8", [8, 8], F32, nxt(32))
    lnst_sb = None

    AR = nxt(0, align=256)
    v_tm = place("v_tm", [128, NTC * E], BF16, AR + 0)
    xn_tmp = place("xn_tmp", [128, 4 * E], BF16, AR + 0)
    sq_sc = place("sq_sc", [128, E], F32, AR + 8192)
    wa_raw = place("wa_raw", [H, E], BF16, AR + 12288)
    mix_fm = place("mix_fm", [128, NEC * S], BF16, AR + 32768)
    z_f32 = place("z_f32", [H, S], F32, AR + 32768)
    z2 = place("z2", [H, S], F32, AR + 32768 + 8192)
    z_bf = place("z_bf", [H, S], BF16, AR + 32768 + 16384)
    wsh = place("wsh", [128, 2 * WSH_F], BF16, AR + 65536)
    if debug:
        lnst_sb = place("lnst_sb", [128, NTC * 8], F32, AR + 32768 + 20480)
    wvt = place("wvt", [128, NEC * E], BF16, AR + 75264)
    wstage = place("wstage", [128, 2 * E], BF16, AR + 91648)
    w1bf = place("w1bf", [128, NEC * DF], BF16, AR + 0)
    a_sb = place("a_sb", [128, NFC * TT], BF16, AR + 65536)
    w2r = place("w2r", [128, 4 * E], BF16, AR + 98304)
    b1_raw = place("b1_raw", [8, 1024], F32, AR + 98304)
    assert AR + 106496 <= TOTAL, (AR, TOTAL)

    ps = [nc.alloc_psum_tensor(f"ps{i}", [128, 512], F32) for i in range(8)]

    prog = {k: [] for k in ("t", "s", "v", "g", "y")}

    from contextlib import ExitStack
    with ExitStack() as ctx:
        sem = {k: Ctr(ctx.enter_context(nc.semaphore(f"sem_{k}")))
               for k in ("ldx", "ldw", "sto", "t", "s", "v", "g")}

        def emit(eng, fn, inc=None, k=1):
            c = sem[inc] if inc else None
            if c is not None:
                c.n += k
            drained = inc in ("s", "v", "g")
            def run(handle, fn=fn, c=c, k=k, drained=drained):
                inst = fn(handle)
                if c is not None:
                    if drained:
                        handle.drain(fusable=True).then_inc(c.h, k)
                    else:
                        inst.then_inc(c.h, k)
            prog[eng].append(run)

        def fence(eng):
            c = sem[eng if eng != "y" else "sto"]
            c.n += 1
            def run(handle, c=c, val=c.n):
                handle.drain(fusable=True).then_inc(c.h, 1)
                handle.wait_ge(c.h, val)
            prog[eng].append(run)

        def wait(eng, which, val):
            c = sem[which]
            assert val <= c.n, f"forward wait {which} {val} > {c.n}"
            if val > 0:
                prog[eng].append(lambda h, c=c, val=val: h.wait_ge(c.h, val))

        def mark(which):
            return sem[which].n

        emit("g", lambda g: g.iota(idx_u[:], [[1, 1]], base=128,
                                   channel_multiplier=-1), inc="g")
        emit("g", lambda g: g.memset(wstage[0:16, 0:512], 0.0), inc="g")
        fence("g")
        emit("g", lambda g: g.dma_start(out=wdr_d.ap()[:, 0:512],
                                        in_=wstage[0:16, 0:512]), inc="ldw", k=16)
        emit("g", lambda g: g.dma_start(out=idn[:], in_=idn_d.ap()), inc="ldw", k=16)
        m_idn = mark("ldw")
        emit("g", lambda g: g.dma_start(out=wa_raw[:], in_=wa_d.ap()), inc="ldw", k=16)
        m_wa = mark("ldw")
        emit("g", lambda g: g.dma_start(
            out=bass.AP(Wg_bf, 0, [[64, 128], [8, 8], [1, 8]]),
            in_=bass.AP(wg_d, 0, [[8, 128], [1024, 8], [1, 8]])), inc="ldw", k=16)
        m_wg = mark("ldw")
        emit("g", lambda g: g.dma_start(out=bo_bf[:],
                                        in_=bass.AP(bo_d, 0, [[E, 1], [1, E]])), inc="ldw", k=16)
        m_bo = mark("ldw")
        emit("g", lambda g: g.dma_start(out=bg_sb[:],
                                        in_=bass.AP(bg_d, 0, [[NE, 1], [1, NE]])), inc="ldw", k=16)
        m_bg = mark("ldw")
        emit("g", lambda g: g.memset(ones8[:], 1.0), inc="g")
        fence("g")
        emit("g", lambda g: g.affine_select(idn8[:], ones8[:], [[1, 8]],
                                            OP.is_equal, 0.0, base=0,
                                            channel_multiplier=-1), inc="g")
        m_idn8 = mark("g")
        emit("g", lambda g: g.dma_start(
            out=b1_raw[:], in_=bass.AP(b1_d, 0, [[DF, 8], [1, 1024]])), inc="ldw", k=16)
        m_b1_piece = [mark("ldw")]
        emit("g", lambda g: g.dma_start(out=b2_bf[:], in_=b2_d.ap()), inc="ldw", k=16)
        m_b2 = mark("ldw")
        m_wv_stage = []
        for fr in range(2):
            emit("g", lambda g, fr=fr: g.dma_start(
                out=wstage[:, (fr % 2) * E:(fr % 2) * E + E],
                in_=wv_d.ap()[128 * fr:128 * fr + 128, :]), inc="ldw", k=16)
            m_wv_stage.append(mark("ldw"))

        emit("v", lambda v: v.memset(ones_bf[:], 1.0), inc="v")
        emit("v", lambda v: v.memset(ones_f[:], 1.0), inc="v")
        emit("v", lambda v: v.memset(sm[:, 20:21], 1e-5), inc="v")
        emit("v", lambda v: v.memset(sm[:, 80:88], 1.0), inc="v")
        fence("v")
        m_ones = mark("v")

        m_x = []
        for tc in range(NTC):
            emit("y", lambda sy, tc=tc: sy.dma_start(
                out=x_acc[:, E * tc:E * tc + E],
                in_=x_d.ap()[128 * tc:128 * tc + 128, :]), inc="ldx", k=16)
            m_x.append(mark("ldx"))

        wait("t", "v", m_ones)
        wait("t", "ldw", m_bg)
        wait("t", "ldw", m_idn)
        emit("t", lambda t: t.matmul(ps[2][:, 0:8], ones_f[0:1, :], bg_sb[0:1, :],
                                     start=True, stop=True), inc="t")
        m_bgbt = mark("t")
        wait("s", "t", m_bgbt)
        emit("s", lambda s_: s_.copy(out=bgb[:], in_=ps[2][:, 0:8]), inc="s")
        m_bgb = mark("s")

        m_xn_tmp = []
        m_xn_tr = []
        m_xn_evac = []
        m_tr_bank = {2: 0, 3: 0, 4: 0, 5: 0, 6: 0, 7: 0}
        m_wv_tr = []
        m_wat = None
        m_b1_tr = []
        m_z_evac = []

        def fill_wa():
            nonlocal m_wat
            wait("t", "ldw", m_wa)
            if m_tr_bank[3]:
                wait("t", "s", m_tr_bank[3])
            for ec in range(NEC):
                emit("t", lambda t, ec=ec: t.transpose(
                    ps[3][:, 8 * ec:8 * ec + 8].bitcast(BF16),
                    wa_raw[:, 128 * ec:128 * ec + 128], idn[0:16, 0:16]),
                    inc="t" if ec == NEC - 1 else None)
            m_watt = mark("t")
            wait("s", "t", m_watt)
            emit("s", lambda s_: s_.copy(out=W_AT[:], in_=ps[3][:, 0:64].bitcast(BF16)), inc="s")
            m_wat = mark("s")
            m_tr_bank[3] = mark("s")

        def fill_wv(fr):
            bank = 4 + fr % 2
            wait("t", "ldw", m_wv_stage[fr])
            if m_tr_bank[bank]:
                wait("t", "s", m_tr_bank[bank])
            for ec in range(NEC):
                emit("t", lambda t, fr=fr, ec=ec, bank=bank: t.transpose(
                    ps[bank][:, 64 * ec:64 * ec + 64].bitcast(BF16),
                    wstage[:, (fr % 2) * E + 128 * ec:(fr % 2) * E + 128 * ec + 128],
                    idn[:]), inc="t" if ec == NEC - 1 else None)
            m_wv_tr.append(mark("t"))
            wait("s", "t", m_wv_tr[fr])
            emit("s", lambda s_, fr=fr, bank=bank: s_.copy(
                out=bass.AP(wvt, 128 * fr, [[NEC * E, 128], [E, NEC], [1, 128]]),
                in_=ps[bank][:].bitcast(BF16)), inc="s")
            m_tr_bank[bank] = mark("s")
            if fr + 2 < NEC:
                wait("g", "t", m_wv_tr[fr])
                emit("g", lambda g, fr=fr: g.dma_start(
                    out=wstage[:, (fr % 2) * E:(fr % 2) * E + E],
                    in_=wv_d.ap()[128 * (fr + 2):128 * (fr + 2) + 128, :]), inc="ldw", k=16)
                m_wv_stage.append(mark("ldw"))

        def fill_b1(q):
            if q == 0:
                wait("t", "g", m_idn8)
                wait("t", "s", m_bgb)
            wait("t", "ldw", m_b1_piece[q])
            if m_tr_bank[2]:
                wait("t", "s", m_tr_bank[2])
            for cc in range(8):
                emit("t", lambda t, cc=cc: t.transpose(
                    ps[2][:, 8 * cc:8 * cc + 8],
                    b1_raw[0:8, 128 * cc:128 * cc + 128], idn8[:]),
                    inc="t" if cc == 7 else None)
            m_b1_tr.append(mark("t"))
            if q + 1 < 4:
                wait("g", "t", m_b1_tr[-1])
                emit("g", lambda g, q=q: g.dma_start(
                    out=b1_raw[:],
                    in_=bass.AP(b1_d, 1024 * (q + 1), [[DF, 8], [1, 1024]])), inc="ldw", k=16)
                m_b1_piece.append(mark("ldw"))
            wait("s", "t", m_b1_tr[-1])
            for cc in range(8):
                emit("s", lambda s_, q=q, cc=cc: s_.copy(
                    out=bass.AP(b1c, 8 * q + cc, [[32 * NE, 128], [32, 8], [1, 1]]),
                    in_=ps[2][:, 8 * cc:8 * cc + 8]), inc="s" if cc == 7 else None)
            m_tr_bank[2] = mark("s")

        fillers = ([fill_wa] + [lambda fr=fr: fill_wv(fr) for fr in range(NEC)]
                   + [lambda q=q: fill_b1(q) for q in range(4)])
        for pp in range(NTC // 2):
            tcs = (2 * pp, 2 * pp + 1)
            cb = (0, 32)
            wait("v", "ldx", m_x[tcs[1]])
            for i in (0, 1):
                xc = x_acc[:, E * tcs[i]:E * tcs[i] + E]
                emit("v", lambda v, xc=xc, b=cb[i]: v.reduce_sum(
                    sm[:, b:b + 1], xc, axis=AX), inc="v" if i else None)
            fence("v")
            wait("s", "ldx", m_x[tcs[1]])
            if pp >= 1:
                wait("s", "v", m_var_prev)
            for i in (0, 1):
                xc = x_acc[:, E * tcs[i]:E * tcs[i] + E]
                emit("s", lambda s_, xc=xc, b=cb[i]: s_.activation(
                    out=sq_sc[:], in_=xc, func=AF.Square,
                    accum_out=sm[:, b + 4:b + 5]), inc="s" if i else None)
            m_sq = mark("s")
            wait("v", "s", m_sq)
            for i in (0, 1):
                b = cb[i]
                emit("v", lambda v, b=b: v.tensor_scalar(sm[:, b + 8:b + 9], sm[:, b:b + 1], 1.0 / E, None, OP.mult))
                emit("v", lambda v, b=b: v.tensor_scalar(sm[:, b + 12:b + 13], sm[:, b + 4:b + 5], 1.0 / E, None, OP.mult))
            fence("v")
            for i in (0, 1):
                b = cb[i]
                emit("v", lambda v, b=b: v.tensor_scalar(
                    sm[:, b + 16:b + 17], sm[:, b + 8:b + 9], sm[:, b + 8:b + 9],
                    sm[:, b + 12:b + 13], OP.mult, OP.subtract), inc="v" if i else None)
            m_var = mark("v")
            m_var_prev = m_var
            wait("s", "v", m_var)
            for i in (0, 1):
                b = cb[i]
                emit("s", lambda s_, b=b: s_.activation(
                    out=sm[:, b + 24:b + 25], in_=sm[:, b + 16:b + 17],
                    func=AF.Sqrt, scale=-1.0, bias=sm[:, 20:21]), inc="s" if i else None)
            m_sd = mark("s")
            wait("v", "s", m_sd)
            for i in (0, 1):
                b = cb[i]
                emit("v", lambda v, b=b: v.reciprocal(sm[:, b + 28:b + 29], sm[:, b + 24:b + 25]), inc="v" if i else None)
            fence("v")
            if debug:
                for i in (0, 1):
                    emit("v", lambda v, tc=tcs[i], b=cb[i]: v.tensor_copy(
                        lnst_sb[:, 8 * tc:8 * tc + 8],
                        bass.AP(sm, b, [[96, 128], [4, 8], [1, 1]])), inc="v")
            if pp >= 2:
                wait("v", "t", m_xn_tr[tcs[0] - 4])
                wait("v", "t", m_xn_tr[tcs[1] - 4])
            for i in (0, 1):
                tc = tcs[i]
                b = cb[i]
                slot = (tc % 4) * E
                xc = x_acc[:, E * tc:E * tc + E]
                emit("v", lambda v, xc=xc, slot=slot, b=b: v.tensor_scalar(
                    xn_tmp[:, slot:slot + E], xc, sm[:, b + 8:b + 9], sm[:, b + 28:b + 29],
                    OP.subtract, OP.mult), inc="v")
                m_xn_tmp.append(mark("v"))
            for i in (0, 1):
                tc = tcs[i]
                slot = (tc % 4) * E
                bank = 6 + tc % 2
                wait("t", "v", m_xn_tmp[tc])
                if tc == 0:
                    wait("t", "ldw", m_idn)
                if m_tr_bank[bank]:
                    wait("t", "s", m_tr_bank[bank])
                for ec in range(NEC):
                    emit("t", lambda t, slot=slot, ec=ec, bank=bank: t.transpose(
                        ps[bank][:, 64 * ec:64 * ec + 64].bitcast(BF16),
                        xn_tmp[:, slot + 128 * ec:slot + 128 * ec + 128], idn[:]),
                        inc="t" if ec == NEC - 1 else None)
                m_xn_tr.append(mark("t"))
                wait("s", "t", m_xn_tr[tc])
                emit("s", lambda s_, tc=tc, bank=bank: s_.copy(
                    out=bass.AP(xn_fm, 128 * tc, [[NEC * S, 128], [S, NEC], [1, 128]]),
                    in_=ps[bank][:].bitcast(BF16)), inc="s")
                m_xn_evac.append(mark("s"))
                m_tr_bank[bank] = mark("s")
                if 1 <= tc <= len(fillers):
                    fillers[tc - 1]()
                if tc % 4 == 3:
                    ts = tc // 4
                    zbank = ts % 2
                    wait("t", "s", m_wat)
                    wait("t", "s", m_xn_evac[tc])
                    if ts >= 2:
                        wait("t", "s", m_z_evac[ts - 2])
                    for ec in range(NEC):
                        emit("t", lambda t, ts=ts, ec=ec, zbank=zbank: t.matmul(
                            ps[zbank][0:16, :],
                            W_AT[:, 16 * ec:16 * ec + 16],
                            xn_fm[:, S * ec + 512 * ts:S * ec + 512 * ts + 512],
                            start=(ec == 0), stop=(ec == NEC - 1)),
                            inc="t" if ec == NEC - 1 else None)
                    m_zt = mark("t")
                    wait("s", "t", m_zt)
                    emit("s", lambda s_, ts=ts, zbank=zbank: s_.activation(
                        out=z_f32[:, 512 * ts:512 * ts + 512], in_=ps[zbank][0:16, :],
                        func=AF.Copy, scale=0.125), inc="s")
                    m_z_evac.append(mark("s"))
        
        m_xn_all = mark("s")
        m_wvt = mark("s")
        m_b1c = mark("s")

        if debug:
            wait("y", "s", m_xn_all)
            emit("y", lambda sy: sy.dma_start(out=taps["xn_fm"].ap(), in_=xn_fm[:]),
                 inc="sto", k=16)
            wait("y", "v", mark("v"))
            emit("y", lambda sy: sy.dma_start(out=taps["lnst"].ap(), in_=lnst_sb[:]),
                 inc="sto", k=16)

        wait("v", "s", m_z_evac[-1])
        emit("v", lambda v: v.reduce_max(sm[0:16, 32:33], z_f32[:], axis=AX, negate=True), inc="v")
        m_zmax = mark("v")
        wait("s", "v", m_zmax)
        emit("s", lambda s_: s_.activation(out=z2[:], in_=z_f32[:], func=AF.Exp,
                                           bias=sm[0:16, 32:33],
                                           accum_out=sm[0:16, 36:37]), inc="s")
        m_zexp = mark("s")
        wait("v", "s", m_zexp)
        emit("v", lambda v: v.reciprocal(sm[0:16, 40:41], sm[0:16, 36:37]), inc="v")
        fence("v")
        m_zrs = mark("v")
        wait("s", "v", m_zrs)
        fence("s")
        emit("s", lambda s_: s_.activation(out=z_bf[:], in_=z2[:], func=AF.Copy,
                                           scale=sm[0:16, 40:41]), inc="s")
        m_zbf = mark("s")
        wait("y", "s", m_zbf)
        emit("y", lambda sy: sy.dma_start(
            out=bass.AP(wdr_d, 512, [[WDRAM, H], [1, S]]), in_=z_bf[:]),
            inc="sto", k=16)
        m_zstore = mark("sto")
        if debug:
            emit("y", lambda sy: sy.dma_start(out=taps["z_bf"].ap(), in_=z_bf[:]),
                 inc="sto", k=16)

        m_lg = []
        m_lg_used = []
        wait("t", "ldw", m_wg)
        wait("t", "s", m_bgb)
        if m_tr_bank[2]:
            wait("t", "s", m_tr_bank[2])
        if m_tr_bank[3]:
            wait("t", "s", m_tr_bank[3])
        for tc in range(NTC):
            bank = 2 + tc % 2
            if tc >= 2:
                wait("t", "v", m_lg_used[tc - 2])
            for ec in range(NEC):
                emit("t", lambda t, tc=tc, ec=ec, bank=bank: t.matmul(
                    ps[bank][:, 0:8],
                    xn_fm[:, S * ec + 128 * tc:S * ec + 128 * tc + 128],
                    Wg_bf[:, 8 * ec:8 * ec + 8],
                    start=(ec == 0), stop=(ec == NEC - 1)),
                    inc="t" if ec == NEC - 1 else None)
            m_lg.append(mark("t"))
            lg = sm[:, 48:56]
            el = sm[:, 64:72]
            sg = sm[:, 32:40]
            top = sm[:, 0:8]
            m28 = sm[:, 8:16]
            den8 = sm[:, 16:24]
            r8 = sm[:, 72:80]
            wait("v", "s", m_bgb)
            wait("v", "t", m_lg[tc])
            emit("v", lambda v, bank=bank, lg=lg: v.tensor_tensor(
                lg, ps[bank][:, 0:8], bgb[:], OP.add), inc="v")
            m_lg_used.append(mark("v"))
            wait("v", "v", m_lg_used[tc])
            emit("v", lambda v, lg=lg, top=top: v.max(top, lg), inc="v")
            fence("v")
            emit("v", lambda v, top=top: v.tensor_scalar(
                sm[:, 88:89], top[:, 0:1], -1.0, None, OP.mult))
            emit("v", lambda v, top=top: v.tensor_scalar(
                sm[:, 92:93], top[:, 1:2], top[:, 2:3], -0.5, OP.add, OP.mult))
            emit("v", lambda v, m28=m28, top=top: v.tensor_scalar(
                m28, sm[:, 80:88], top[:, 1:2], None, OP.mult), inc="v")
            m_pre = mark("v")
            wait("s", "v", m_pre)
            emit("s", lambda s_, lg=lg, el=el: s_.activation(
                out=el, in_=lg, func=AF.Exp, bias=sm[:, 88:89]))
            emit("s", lambda s_, lg=lg, sg=sg: s_.activation(
                out=sg, in_=lg, func=AF.Sign, bias=sm[:, 92:93]))
            emit("s", lambda s_, m28=m28, den8=den8: s_.activation(
                out=den8, in_=m28, func=AF.Exp, bias=sm[:, 88:89]), inc="s")
            m_exps = mark("s")
            wait("v", "s", m_exps)
            emit("v", lambda v, sg=sg: v.tensor_scalar(sg, sg, 1.0, 0.5, OP.add, OP.mult))
            emit("v", lambda v, den8=den8: v.tensor_scalar(den8, den8, 1.0, None, OP.add), inc="v")
            fence("v")
            emit("v", lambda v, el=el, sg=sg: v.tensor_tensor(el, el, sg, OP.mult))
            emit("v", lambda v, r8=r8, den8=den8: v.reciprocal(r8, den8), inc="v")
            fence("v")
            emit("v", lambda v, tc=tc, el=el, r8=r8: v.tensor_tensor(
                comb[:, 8 * tc:8 * tc + 8], el, r8, OP.mult), inc="v")
        m_comb_all = mark("v")
        if debug:
            wait("y", "v", m_comb_all)
            emit("y", lambda sy: sy.dma_start(out=taps["comb"].ap(), in_=comb[:]),
                 inc="sto", k=16)


        wait("t", "s", m_wvt)
        wait("t", "s", m_z_evac[-1])
        m_v_evac = []
        for tc in range(NTC):
            for fh in range(2):
                bank = (2 * tc + fh) % 2
                grp = 2 * tc + fh
                if grp >= 2:
                    wait("t", "s", m_v_evac[grp - 2])
                for ec in range(NEC):
                    emit("t", lambda t, tc=tc, fh=fh, ec=ec, bank=bank: t.matmul(
                        ps[bank][:],
                        xn_fm[:, S * ec + 128 * tc:S * ec + 128 * tc + 128],
                        wvt[:, E * ec + 512 * fh:E * ec + 512 * fh + 512],
                        start=(ec == 0), stop=(ec == NEC - 1)),
                        inc="t" if ec == NEC - 1 else None)
                m_vt = mark("t")
                wait("s", "t", m_vt)
                emit("s", lambda s_, tc=tc, fh=fh, bank=bank: s_.copy(
                    out=v_tm[:, E * tc + 512 * fh:E * tc + 512 * fh + 512],
                    in_=ps[bank][:]), inc="s")
                m_v_evac.append(mark("s"))
        m_v_all = mark("s")
        if debug:
            wait("y", "s", m_v_all)
            emit("y", lambda sy: sy.dma_start(out=taps["v_tm"].ap(), in_=v_tm[:]),
                 inc="sto", k=16)

        m_wo_tr = []
        for fr in range(NEC):
            if fr < 2:
                wait("g", "t", m_wv_tr[-1])
            else:
                wait("g", "t", m_wo_tr[fr - 2])
            emit("g", lambda g, fr=fr: g.dma_start(
                out=wstage[:, (fr % 2) * E:(fr % 2) * E + E],
                in_=wo_d.ap()[128 * fr:128 * fr + 128, :]), inc="ldw", k=16)
            m_stage = mark("ldw")
            wait("t", "ldw", m_stage)
            wait("t", "s", m_v_all)
            if m_tr_bank[7]:
                wait("t", "s", m_tr_bank[7])
            for ec in range(NEC):
                emit("t", lambda t, fr=fr, ec=ec: t.transpose(
                    ps[7][:, 64 * ec:64 * ec + 64].bitcast(BF16),
                    wstage[:, (fr % 2) * E + 128 * ec:(fr % 2) * E + 128 * ec + 128],
                    idn[:]), inc="t" if ec == NEC - 1 else None)
            m_wo_tr.append(mark("t"))
            wait("s", "t", m_wo_tr[fr])
            emit("s", lambda s_, fr=fr: s_.copy(
                out=bass.AP(wvt, 128 * fr, [[NEC * E, 128], [E, NEC], [1, 128]]),
                in_=ps[7][:].bitcast(BF16)), inc="s")
            m_tr_bank[7] = mark("s")
        m_wot = mark("s")

        m_gather = {}
        m_sub_done = {}
        m_mix_evac = []
        mix_banks = [2, 3, 4, 5]
        for pr in range(8):
            for sub in range(2):
                h = 2 * pr + sub
                if h < 2:
                    wait("g", "sto", m_zstore)
                else:
                    wait("g", "t", m_sub_done[(pr - 1, sub)])
                emit("g", lambda g, h=h: g.indirect_dma_start(
                    out=wsh[:, (h % 2) * WSH_F:(h % 2) * WSH_F + WSH_F],
                    out_offset=None,
                    in_=bass.AP(wdr_d, 0, [[1, 1], [1, WDRAM]]),
                    in_offset=IndirectOffsetOnAxis(idx_u[:], 1),
                    element_offset=WDRAM * h), inc="ldw", k=16)
                m_gather[h] = mark("ldw")
            for sub in range(2):
                h = 2 * pr + sub
                wait("t", "ldw", m_gather[h])
                if pr == 0 and sub == 0:
                    wait("t", "s", m_v_all)
                    wait("t", "v", m_lg_used[-1])
                for it in range(4):
                    bank = mix_banks[it]
                    if pr >= 1 and sub == 0:
                        wait("t", "s", m_mix_evac[4 * (pr - 1) + it])
                    i0 = 512 * it
                    jmax = min(NTC - 1, (i0 + 511) // 128)
                    for jc in range(jmax + 1):
                        F0 = i0 - 128 * jc + 384
                        last = (jc == jmax)
                        emit("t", lambda t, h=h, jc=jc, F0=F0, sub=sub, bank=bank,
                             first=(jc == 0), last=last: t.matmul(
                            ps[bank][64 * sub:64 * sub + 64, :],
                            v_tm[:, E * jc + 64 * h:E * jc + 64 * h + 64],
                            wsh[:, (h % 2) * WSH_F + F0:(h % 2) * WSH_F + F0 + 512],
                            start=first, stop=last),
                            inc="t" if last else None)
                m_sub_done[(pr, sub)] = mark("t")
            for it in range(4):
                bank = mix_banks[it]
                wait("s", "t", m_sub_done[(pr, 1)])
                emit("s", lambda s_, pr=pr, it=it, bank=bank: s_.copy(
                    out=mix_fm[:, S * pr + 512 * it:S * pr + 512 * it + 512],
                    in_=ps[bank][:]), inc="s")
                m_mix_evac.append(mark("s"))
        m_mix_last_t = mark("t")
        m_mix_all = mark("s")
        if debug:
            wait("y", "s", m_mix_all)
            emit("y", lambda sy: sy.dma_start(out=taps["mix"].ap(), in_=mix_fm[:]),
                 inc="sto", k=16)

        m_b2add = []
        wait("t", "v", m_comb_all)
        wait("t", "s", m_mix_all)
        if m_tr_bank[6]:
            wait("t", "s", m_tr_bank[6])
        wait("t", "ldw", m_b2)
        m_cbf_prev = 0
        for tc in range(NTC):
            if m_cbf_prev:
                wait("s", "t", m_cbf_prev)
            emit("s", lambda s_, tc=tc: s_.copy(
                out=comb_bf[:], in_=comb[:, 8 * tc:8 * tc + 8]), inc="s")
            m_cbf = mark("s")
            wait("t", "s", m_cbf)
            emit("t", lambda t: t.transpose(ps[6][0:8, 0:64].bitcast(BF16),
                                            comb_bf[:], idn[:]), inc="t")
            m_ctr = mark("t")
            m_cbf_prev = m_ctr
            wait("s", "t", m_ctr)
            emit("s", lambda s_: s_.copy(out=combT[:],
                                         in_=ps[6][0:8, 0:64].bitcast(BF16)), inc="s")
            m_ctb = mark("s")
            wait("t", "s", m_ctb)
            for dh in range(2):
                bank = 2 + dh
                if tc >= 1:
                    wait("t", "v", m_b2add[tc - 1][dh])
                emit("t", lambda t, dh=dh, bank=bank: t.matmul(
                    ps[bank][:], combT[:], b2_bf[:, 512 * dh:512 * dh + 512],
                    start=True, stop=True), inc="t")
                m_b2t = mark("t")
                wait("v", "t", m_b2t)
                dst = x_acc[:, E * tc + 512 * dh:E * tc + 512 * dh + 512]
                emit("v", lambda v, dst=dst, bank=bank: v.tensor_tensor(
                    dst, dst, ps[bank][:], OP.add), inc="v")
                if dh == 0:
                    m_b2add.append([mark("v"), None])
                else:
                    m_b2add[tc][1] = mark("v")

        wait("t", "s", m_wot)
        wait("t", "ldw", m_bo)
        wait("t", "s", m_mix_all)
        m_wo_add = []
        for tc in range(NTC):
            for gh in range(2):
                bank = (2 * tc + gh) % 2
                grp = 2 * tc + gh
                if grp >= 2:
                    wait("t", "v", m_wo_add[grp - 2])
                for fc in range(NEC):
                    emit("t", lambda t, tc=tc, gh=gh, fc=fc, bank=bank: t.matmul(
                        ps[bank][:],
                        mix_fm[:, S * fc + 128 * tc:S * fc + 128 * tc + 128],
                        wvt[:, E * fc + 512 * gh:E * fc + 512 * gh + 512],
                        start=(fc == 0), stop=False))
                emit("t", lambda t, gh=gh, bank=bank: t.matmul(
                    ps[bank][:], ones_bf[0:1, :],
                    bo_bf[0:1, 512 * gh:512 * gh + 512],
                    start=False, stop=True), inc="t")
                m_wt = mark("t")
                wait("v", "t", m_wt)
                dst = x_acc[:, E * tc + 512 * gh:E * tc + 512 * gh + 512]
                emit("v", lambda v, dst=dst, bank=bank: v.tensor_tensor(
                    dst, dst, ps[bank][:], OP.add), inc="v")
                m_wo_add.append(mark("v"))
        m_attn_done = mark("v")
        m_wo_last_t = mark("t")
        if debug:
            wait("y", "v", m_attn_done)
            emit("y", lambda sy: sy.dma_start(out=taps["acc_attn"].ap(), in_=x_acc[:]),
                 inc="sto", k=16)

        m_w1blk = {}
        def emit_w1_load_init(e_):
            wait("g", "t", m_mix_last_t)
            for dc in range(4):
                emit("g", lambda g, e_=e_, dc=dc: g.dma_start(
                    out=w1bf[:, DF * dc:DF * dc + DF],
                    in_=w1_d.ap()[e_, 128 * dc:128 * dc + 128, :]), inc="ldw", k=16)
            m_w1blk[(e_, "lo")] = mark("ldw")
            wait("g", "t", m_wo_last_t)
            m_w1_dc = {}
            for dc in range(4, 8):
                emit("g", lambda g, e_=e_, dc=dc: g.dma_start(
                    out=w1bf[:, DF * dc:DF * dc + DF],
                    in_=w1_d.ap()[e_, 128 * dc:128 * dc + 128, :]), inc="ldw", k=16)
                m_w1_dc[dc] = mark("ldw")
            m_w1blk[e_] = [mark("ldw")] * 8
            m_w1blk[(e_, "dc")] = m_w1_dc

        def emit_w1_blocks(e_, m_h_fc):
            marks = []
            for blk in range(8):
                wait("g", "t", m_h_fc[4 * blk + 3])
                for dc in range(NEC):
                    emit("g", lambda g, e_=e_, blk=blk, dc=dc: g.dma_start(
                        out=w1bf[:, DF * dc + 512 * blk:DF * dc + 512 * blk + 512],
                        in_=bass.AP(w1_d, (e_ * E + 128 * dc) * DF + 512 * blk,
                                    [[DF, 128], [1, 512]])), inc="ldw", k=16)
                marks.append(mark("ldw"))
            m_w1blk[e_] = marks

        emit_w1_load_init(E_SEQ[0])

        m_a_evac = {}
        m_y_bank_v = [m_attn_done] * 8
        m_h_bank_s = [0, 0]
        m_y_stop = [0] * 8
        m_tt_final = [None] * NTT
        m_w2_slot_t = [m_wo_last_t] * 3
        m_w2stash = {}
        slot_seq = 0

        for ei, e_ in enumerate(E_SEQ[:n_experts]):
            act_fn = AF.Gelu if e_ % 2 == 0 else AF.Silu
            for tt in range(NTT):
                m_h_fc = []
                for fc in range(NFC):
                    bank = fc % 2
                    if m_h_bank_s[bank]:
                        wait("t", "s", m_h_bank_s[bank])
                    if fc % 4 == 0:
                        if ei == 0 and tt == 0 and fc == 0:
                            pass
                        else:
                            wait("t", "ldw", m_w1blk[e_][fc // 4])
                    if fc == 0:
                        wait("t", "v", m_y_bank_v[0])
                    if fc == 1:
                        wait("t", "v", m_y_bank_v[1])
                    for dc in range(NEC):
                        if ei == 0 and tt == 0 and fc == 0:
                            if dc < 4:
                                pass
                            else:
                                wait("t", "ldw", m_w1blk[(e_, "dc")][dc])
                            if dc == 0:
                                wait("t", "ldw", m_w1blk[(e_, "lo")])
                        emit("t", lambda t, tt=tt, fc=fc, dc=dc, bank=bank: t.matmul(
                            ps[bank][:],
                            w1bf[:, DF * dc + 128 * fc:DF * dc + 128 * fc + 128],
                            xn_fm[:, S * dc + TT * tt:S * dc + TT * tt + TT],
                            start=(dc == 0), stop=(dc == NEC - 1)),
                            inc="t" if dc == NEC - 1 else None)
                    m_h = mark("t")
                    m_h_fc.append(m_h)
                    wait("s", "t", m_h)
                    emit("s", lambda s_, e_=e_, fc=fc, bank=bank, act_fn=act_fn: s_.activation(
                        out=a_sb[:, TT * fc:TT * fc + TT], in_=ps[bank][:],
                        func=act_fn, bias=b1c[:, 32 * e_ + fc:32 * e_ + fc + 1]), inc="s")
                    m_a_evac[fc] = mark("s")
                    m_h_bank_s[bank] = mark("s")
                if debug and ei == 0 and tt == 0:
                    wait("y", "s", m_a_evac[NFC - 1])
                    emit("y", lambda sy: sy.dma_start(out=taps["a_e0"].ap(), in_=a_sb[:]),
                         inc="sto", k=16)
                for fc in range(NFC):
                    slot = slot_seq % 3
                    slot_seq += 1
                    if tt == 0:
                        wait("g", "t", m_w2_slot_t[slot])
                        if m_w2stash.get((e_, fc - 3)) is not None:
                            wait("g", "sto", m_w2stash[(e_, fc - 3)])
                        emit("g", lambda g, e_=e_, fc=fc, slot=slot: g.dma_start(
                            out=w2r[:, E * slot:E * slot + E],
                            in_=w2_d.ap()[e_, 128 * fc:128 * fc + 128, :]), inc="ldw", k=16)
                        m_w2f = ("ldw", mark("ldw"))
                        wait("y", "ldw", m_w2f[1])
                        emit("y", lambda sy, e_=e_, fc=fc, slot=slot: sy.dma_start(
                            out=w2bf_d.ap()[e_, 128 * fc:128 * fc + 128, :],
                            in_=w2r[:, E * slot:E * slot + E]), inc="sto", k=16)
                        m_w2stash[(e_, fc)] = mark("sto")
                    else:
                        wait("y", "t", m_w2_slot_t[slot])
                        if tt == 1:
                            wait("y", "sto", m_w2stash[(e_, fc)])
                        emit("y", lambda sy, e_=e_, fc=fc, slot=slot: sy.dma_start(
                            out=w2r[:, E * slot:E * slot + E],
                            in_=w2bf_d.ap()[e_, 128 * fc:128 * fc + 128, :]), inc="ldx", k=16)
                        m_w2f = ("ldx", mark("ldx"))
                    wait("t", m_w2f[0], m_w2f[1])
                    if fc == 0:
                        wait("t", "s", m_a_evac[0])
                    if fc == 0:
                        for b in range(2, 8):
                            wait("t", "v", m_y_bank_v[b])
                    tc4dh = ([(1, 0), (1, 1), (2, 0), (2, 1), (3, 0), (3, 1),
                              (0, 0), (0, 1)] if fc == 0 else
                             [(a, b) for a in range(4) for b in range(2)])
                    nmm = 0
                    for tc4, dh in tc4dh:
                        bank = 2 * tc4 + dh
                        nmm += 1
                        if fc == 0 and bank == 0:
                            wait("t", "s", m_a_evac[NFC - 1])
                        emit("t", lambda t, fc=fc, tc4=tc4, dh=dh, slot=slot, bank=bank: t.matmul(
                            ps[bank][:],
                            a_sb[:, TT * fc + 128 * tc4:TT * fc + 128 * tc4 + 128],
                            w2r[:, E * slot + 512 * dh:E * slot + 512 * dh + 512],
                            start=(fc == 0), stop=(fc == NFC - 1)),
                            inc="t" if (fc == NFC - 1 or nmm == 8) else None)
                        if fc == NFC - 1:
                            m_y_stop[bank] = mark("t")
                    m_w2_slot_t[slot] = mark("t")
                m_y = mark("t")
                for tc4 in range(4):
                    tc = 4 * tt + tc4
                    for dh in range(2):
                        bank = 2 * tc4 + dh
                        wait("v", "t", m_y_stop[bank])
                        dst = x_acc[:, E * tc + 512 * dh:E * tc + 512 * dh + 512]
                        emit("v", lambda v, dst=dst, bank=bank, tc=tc, e_=e_: v.scalar_tensor_tensor(
                            out=dst, in0=ps[bank][:],
                            scalar=comb[:, 8 * tc + e_:8 * tc + e_ + 1],
                            in1=dst, op0=OP.mult, op1=OP.add), inc="v")
                        m_y_bank_v[bank] = mark("v")
                if ei == n_experts - 1:
                    m_tt_final[tt] = mark("v")
                if tt == NTT - 1 and ei + 1 < n_experts:
                    emit_w1_blocks(E_SEQ[ei + 1], m_h_fc)
        m_final = mark("v")

        for q in range(4):
            wait("y", "v", m_tt_final[q] if m_tt_final[q] is not None else m_final)
            emit("y", lambda sy, q=q: sy.dma_start(
                out=bass.AP(out_d, 128 * E * 4 * q, [[E, 128], [128 * E, 4], [1, E]]),
                in_=x_acc[:, 4 * E * q:4 * E * q + 4 * E]), inc="sto", k=16)
        m_out = mark("sto")
        wait("y", "sto", m_out)

        with nc.Block() as block:
            @block.tensor
            def _(t):
                for fn in prog["t"]:
                    fn(t)

            @block.scalar
            def _(s_):
                for fn in prog["s"]:
                    fn(s_)

            @block.vector
            def _(v):
                for fn in prog["v"]:
                    fn(v)

            @block.gpsimd
            def _(g):
                for fn in prog["g"]:
                    fn(g)

            @block.sync
            def _(sy):
                for fn in prog["y"]:
                    fn(sy)

    return nc


_IDN = np.eye(128, dtype=np.float32).astype(ml_dtypes.bfloat16)


def kernel(L0_attn_x, L0_attn_W_A, L0_attn_W_V, L0_attn_W_O, L0_attn_b_O,
           L0_ffn_W1, L0_ffn_b1, L0_ffn_W2, L0_ffn_b2, L0_ffn_Wg, L0_ffn_bg):
    x = np.ascontiguousarray(np.asarray(L0_attn_x, dtype=np.float32))
    shared = {
        "W_A": np.ascontiguousarray(np.asarray(L0_attn_W_A, dtype=np.float32)),
        "W_V": np.ascontiguousarray(np.asarray(L0_attn_W_V, dtype=np.float32)),
        "W_O": np.ascontiguousarray(np.asarray(L0_attn_W_O, dtype=np.float32)),
        "b_O": np.ascontiguousarray(np.asarray(L0_attn_b_O, dtype=np.float32)),
        "W1": np.ascontiguousarray(np.asarray(L0_ffn_W1, dtype=np.float32)),
        "b1": np.ascontiguousarray(np.asarray(L0_ffn_b1, dtype=np.float32)),
        "W2": np.ascontiguousarray(np.asarray(L0_ffn_W2, dtype=np.float32)),
        "b2": np.ascontiguousarray(np.asarray(L0_ffn_b2, dtype=np.float32)),
        "Wg": np.ascontiguousarray(np.asarray(L0_ffn_Wg, dtype=np.float32)),
        "bg": np.ascontiguousarray(np.asarray(L0_ffn_bg, dtype=np.float32)),
        "idn": _IDN,
    }
    nc = build(debug=False)
    in_maps = [dict(shared, x=np.ascontiguousarray(x[c])) for c in range(8)]
    res = run_bass_kernel_spmd(nc, in_maps, core_ids=list(range(8)))
    return np.stack([np.asarray(r["out"]) for r in res.results], axis=0)
